# version 04
# baseline (speedup 1.0000x reference)
import contextlib
import numpy as np

import concourse.bass as bass
import concourse.mybir as mybir
from concourse.bass_utils import run_bass_kernel_spmd

F32 = mybir.dt.float32
BF16 = mybir.dt.bfloat16
AF = mybir.ActivationFunctionType
ALU = mybir.AluOpType
AX = mybir.AxisListType

N = 4096
D = 2048
NH = 16
NEXP = 16
DI = 1408
NJ = 11
NT = 32
G = 4
GT = 8
LN_EPS = 1e-5
AT_EPS = 1e-6
RD = 1.0 / D

ENGINES = ("sy", "gp", "pe", "dv", "ac")
_DBG = False


class Prog:

    def __init__(self, nc):
        self.nc = nc
        self.cnt = {}
        self.handles = {}
        self.waited = {e: {} for e in ENGINES}
        self.items = {e: [] for e in ENGINES}
        self._stack = contextlib.ExitStack()

    def sem(self, name):
        self.handles[name] = self._stack.enter_context(self.nc.semaphore(name))
        self.cnt[name] = 0

    def wait(self, eng, dep):
        if dep is None:
            return
        s, v = dep
        if self.waited[eng].get(s, 0) >= v:
            return
        self.waited[eng][s] = v
        h = self.handles[s]
        self.items[eng].append(("w", h, v))

    def step(self, eng, fn, waits=(), inc=None, k=1):
        for d in waits:
            self.wait(eng, d)
        tag = None
        if inc is not None:
            self.cnt[inc] += k
            tag = (inc, self.cnt[inc])
            self.items[eng].append(("i", fn, self.handles[inc], k))
        else:
            self.items[eng].append(("i", fn, None, 0))
        return tag

    def dv(self, fn, *waits):
        return self.step("dv", fn, waits, inc="DV")

    def ac(self, fn, *waits):
        return self.step("ac", fn, waits, inc="AC")

    def gc(self, fn, *waits):
        return self.step("gp", fn, waits, inc="GC")

    def pe(self, fn, *waits, tag=True):
        return self.step("pe", fn, waits, inc="PE" if tag else None)

    def dma(self, q, fn, *waits):
        semname = {"sy": "SY", "gp": "GP", "ac": "AD"}[q]
        return self.step(q, fn, waits, inc=semname, k=16)

    def barrier(self, engines=("pe", "dv", "ac", "sy")):
        snap = [("PE", self.cnt["PE"]), ("DV", self.cnt["DV"]),
                ("AC", self.cnt["AC"]), ("SY", self.cnt["SY"]),
                ("AD", self.cnt["AD"])]
        for e in engines:
            for d in snap:
                self.wait(e, d)

    def emit_block(self):
        items = self.items
        self.items = {e: [] for e in ENGINES}
        with self.nc.Block() as block:
            def runner(eng_key):
                def run(E):
                    for it in items[eng_key]:
                        if it[0] == "w":
                            E.wait_ge(it[1], it[2])
                        else:
                            ins = it[1](E)
                            if it[2] is not None:
                                ins.then_inc(it[2], it[3])
                return run
            block.sync(runner("sy"))
            block.gpsimd(runner("gp"))
            block.tensor(runner("pe"))
            block.vector(runner("dv"))
            block.scalar(runner("ac"))


def r2(ap3):
    return ap3.rearrange("p a b -> p (a b)")


def build(nc=None, upto='all'):
    nc = nc or bass.Bass("TRN2", target_bir_lowering=False)
    p = Prog(nc)
    for s in ("SY", "GP", "AD", "PE", "DV", "AC", "GC"):
        p.sem(s)

    x_d = nc.dram_tensor("x", [N, D], F32, kind="ExternalInput")
    Wq_d = nc.dram_tensor("Wq", [D, D], F32, kind="ExternalInput")
    Wk_d = nc.dram_tensor("Wk", [D, D], F32, kind="ExternalInput")
    Wv_d = nc.dram_tensor("Wv", [D, D], F32, kind="ExternalInput")
    Wo_d = nc.dram_tensor("Wo", [D, D], F32, kind="ExternalInput")
    bq_d = nc.dram_tensor("bq", [D], F32, kind="ExternalInput")
    bk_d = nc.dram_tensor("bk", [D], F32, kind="ExternalInput")
    bv_d = nc.dram_tensor("bv", [D], F32, kind="ExternalInput")
    bo_d = nc.dram_tensor("bo", [D], F32, kind="ExternalInput")
    gw_d = nc.dram_tensor("gw", [NEXP, D], F32, kind="ExternalInput")
    w1_d = nc.dram_tensor("w1", [NEXP, DI, D], F32, kind="ExternalInput")
    w2_d = nc.dram_tensor("w2", [NEXP, D, DI], F32, kind="ExternalInput")
    w3_d = nc.dram_tensor("w3", [NEXP, DI, D], F32, kind="ExternalInput")
    sw1_d = nc.dram_tensor("sw1", [2 * DI, D], F32, kind="ExternalInput")
    sw2_d = nc.dram_tensor("sw2", [D, 2 * DI], F32, kind="ExternalInput")
    sw3_d = nc.dram_tensor("sw3", [2 * DI, D], F32, kind="ExternalInput")
    out_d = nc.dram_tensor("out", [N, D], F32, kind="ExternalOutput")

    cWq = nc.dram_tensor("cWq", [D, D], BF16)
    cWk = nc.dram_tensor("cWk", [D, D], BF16)
    cWv = nc.dram_tensor("cWv", [D, D], BF16)
    cWo = nc.dram_tensor("cWo", [D, D], BF16)
    cgw = nc.dram_tensor("cgw", [NEXP, D], BF16)
    cw1 = nc.dram_tensor("cw1", [NEXP, DI, D], BF16)
    cw2 = nc.dram_tensor("cw2", [NEXP, D, DI], BF16)
    cw3 = nc.dram_tensor("cw3", [NEXP, DI, D], BF16)
    csw1 = nc.dram_tensor("csw1", [2 * DI, D], BF16)
    csw2 = nc.dram_tensor("csw2", [D, 2 * DI], BF16)
    csw3 = nc.dram_tensor("csw3", [2 * DI, D], BF16)
    tw13 = nc.dram_tensor("tw13", [18, 2, 6, 128, 16, 256], BF16)
    tw2q = nc.dram_tensor("tw2q", [18, 4, 128, 11, 512], BF16)
    tWa = nc.dram_tensor("tWa", [4, 128, 16, 2048], BF16)
    Tb = nc.dram_tensor("Tb", [N, D], BF16)
    Ob = nc.dram_tensor("Ob", [N, D], BF16)
    Tf = nc.dram_tensor("Tf", [N, D], BF16)
    X1 = nc.dram_tensor("X1", [N, D], F32)

    dbg = {}
    if _DBG:
        dbg["kvacc"] = nc.dram_tensor("d_kvacc", [128, 18 * 132], F32,
                                      kind="ExternalOutput")
        dbg["x1"] = nc.dram_tensor("d_x1", [N, D], F32, kind="ExternalOutput")
        dbg["tf"] = nc.dram_tensor("d_tf", [N, D], BF16, kind="ExternalOutput")
        dbg["we"] = nc.dram_tensor("d_we", [128, 256], F32,
                                   kind="ExternalOutput")
        dbg["qrt"] = nc.dram_tensor("d_qrt", [128, 16 * 1024], BF16,
                                    kind="ExternalOutput")
        dbg["y"] = nc.dram_tensor("d_y", [128, 16 * 2048], F32,
                                  kind="ExternalOutput")
        dbg["h"] = nc.dram_tensor("d_h", [128, 11 * 1024], BF16,
                                  kind="ExternalOutput")
        dbg["ttf"] = nc.dram_tensor("d_ttf", [128, 16 * 1024], BF16,
                                    kind="ExternalOutput")

    ges = contextlib.ExitStack()

    def galloc(name, shape, dt):
        return ges.enter_context(nc.sbuf_tensor(name, shape, dt))

    stg2 = galloc("stg2", [128, 2, 2048], BF16)
    epsc = galloc("epsc", [128, 1], F32)
    psA = ges.enter_context(nc.psum_tensor("psA", [128, 4096], F32))

    attn_es = contextlib.ExitStack()

    def aalloc(name, shape, dt):
        return attn_es.enter_context(nc.sbuf_tensor(name, shape, dt))

    stg = aalloc("stg", [128, 2, 1024], BF16)
    rows_bf = aalloc("rows_bf", [1, 4096 + 2048 + 128], BF16)
    biasrow_kv = rows_bf[:, 0:4096]
    biasrow_o = rows_bf[:, 4096:6144]
    ones_bf = rows_bf[:, 6144:6272]
    bq_nat = aalloc("bq_nat", [16, 128], F32)
    bq_pp = aalloc("bq_pp", [128, 16], F32)
    ident16 = aalloc("ident16", [16, 16], F32)
    decay = aalloc("decay", [128, 32], F32)
    KVS = aalloc("KVS", [128, 18, 132], BF16)

    def bank(i, off=0, n=512):
        return psA[:, i * 512 + off:i * 512 + off + n]

    def kvslot(h):
        base = (h // 3) * 512 + (h % 3) * 132
        return psA[:, base:base + 129]

    def ln_stats(src, lns, sqd, deps, b_war):
        a = p.dv(lambda E: E.reduce_sum(lns[:, 0:1], src, axis=AX.X), *deps)
        b = p.ac(lambda E: E.activation(sqd[:, :], src, AF.Square,
                                        accum_out=lns[:, 1:2]), *deps, b_war)
        c1 = p.dv(lambda E: E.tensor_scalar(lns[:, 2:3], lns[:, 0:1], RD,
                                            None, ALU.mult), a)
        d1 = p.dv(lambda E: E.tensor_scalar(lns[:, 3:4], lns[:, 1:2], RD,
                                            None, ALU.mult), b)
        e1 = p.dv(lambda E: E.tensor_tensor(lns[:, 4:5], lns[:, 2:3],
                                            lns[:, 2:3], ALU.mult), c1)
        f1 = p.dv(lambda E: E.tensor_tensor(lns[:, 4:5], lns[:, 3:4],
                                            lns[:, 4:5], ALU.subtract), d1, e1)
        g1 = p.ac(lambda E: E.activation(lns[:, 5:6], lns[:, 4:5], AF.Sqrt,
                                         bias=epsc[:, :]), f1, t_eps)
        h1 = p.dv(lambda E: E.reciprocal(lns[:, 6:7], lns[:, 5:6]), g1)
        return c1, h1, d1, b

    def conv(dst_dram, src_dram, rows, cols):
        jobs = []
        for a in range(rows // 128):
            for c0 in range(0, cols, 1024):
                c1 = min(c0 + 1024, cols)
                jobs.append((src_dram[a * 128:(a + 1) * 128, c0:c1],
                             dst_dram[a * 128:(a + 1) * 128, c0:c1], c1 - c0))
        return jobs

    conv_seq = [("Wk", conv(cWk, Wk_d, D, D)), ("Wv", conv(cWv, Wv_d, D, D)),
                ("Wq", conv(cWq, Wq_d, D, D)), ("Wo", conv(cWo, Wo_d, D, D))]
    for e in range(NEXP):
        conv_seq.append((f"e{e}", conv(cw1[e], w1_d[e], DI, D)
                         + conv(cw3[e], w3_d[e], DI, D)
                         + conv(cw2[e], w2_d[e], D, DI)))
    conv_seq.append(("e16", conv(csw1[0:DI], sw1_d[0:DI], DI, D)
                     + conv(csw3[0:DI], sw3_d[0:DI], DI, D)))
    conv_seq.append(("e17", conv(csw1[DI:2 * DI], sw1_d[DI:2 * DI], DI, D)
                     + conv(csw3[DI:2 * DI], sw3_d[DI:2 * DI], DI, D)
                     + conv(csw2, sw2_d, D, 2 * DI)))

    conv_state = {"i": 0, "pend": [None, None], "tags": {}}

    def run_conversions(keys):
        for key, jobs in conv_seq:
            if key not in keys:
                continue
            last = None
            for (src, dst, cols) in jobs:
                i = conv_state["i"]
                buf = stg[:, i % 2, 0:cols]
                rd = p.dma("gp", lambda E, b=buf, s=src: E.dma_start(b, s),
                           conv_state["pend"][i % 2])
                wr = p.dma("gp", lambda E, b=buf, d=dst: E.dma_start(d, b), rd)
                conv_state["pend"][i % 2] = wr
                conv_state["i"] += 1
                last = wr
            conv_state["tags"][key] = last

    def conv_tag(key):
        return conv_state["tags"][key]

    def w1src(e, r0, r1, c0, c1):
        if e < NEXP:
            return cw1[e][r0:r1, c0:c1]
        return csw1[(e - NEXP) * DI + r0:(e - NEXP) * DI + r1, c0:c1]

    def w3src(e, r0, r1, c0, c1):
        if e < NEXP:
            return cw3[e][r0:r1, c0:c1]
        return csw3[(e - NEXP) * DI + r0:(e - NEXP) * DI + r1, c0:c1]

    def w2src(e, r0, r1, c0, c1):
        if e < NEXP:
            return cw2[e][r0:r1, c0:c1]
        return csw2[r0:r1, (e - NEXP) * DI + c0:(e - NEXP) * DI + c1]

    def e_fill_tag(e):
        return fill_tag(f"e{e}")

    fill_state = {"i": 0, "pend": [None, None], "tags": {}}

    def _fill_job(tr_src, writes, dep):
        i = fill_state["i"]
        fill_state["i"] += 1
        buf = stg2[:, i % 2, :]
        rd = p.dma("sy", lambda E, b=buf, s=tr_src: E.dma_start(
            b[:, 0:s.shape[0]], s, transpose=True),
            dep, fill_state["pend"][i % 2])
        last = None
        for (dst, sl) in writes:
            last = p.dma("gp", lambda E, b=buf, d=dst, sl=sl: E.dma_start(
                d, sl(b)), rd)
        fill_state["pend"][i % 2] = last
        return last

    def run_fills(keys):
        for key in keys:
            dep = conv_tag(key)
            last = None
            if key in ("Wq", "Wk", "Wv", "Wo"):
                w = {"Wq": 0, "Wk": 1, "Wv": 2, "Wo": 3}[key]
                src = {"Wq": cWq, "Wk": cWk, "Wv": cWv, "Wo": cWo}[key]
                for c in range(16):
                    last = _fill_job(
                        src[:, c * 128:(c + 1) * 128],
                        [(tWa[w][:, c, :], lambda b: b[:, 0:2048])], dep)
            else:
                e = int(key[1:])
                if e == 16 and "e17" in conv_state["tags"]:
                    dep = conv_tag("e17")
                for w, srcfn in ((0, w1src), (1, w3src)):
                    for c in range(16):
                        dst = tw13[e, w].rearrange(
                            "b p c j -> p c b j")[:, c, :, :]
                        last = _fill_job(
                            srcfn(e, 0, DI, c * 128, (c + 1) * 128),
                            [(dst, lambda b: b[:, 0:1536].rearrange(
                                "p (b j) -> p b j", j=256))], dep)
                for j in range(NJ):
                    dst = tw2q[e].rearrange("q p j d -> p j q d")[:, j, :, :]
                    last = _fill_job(
                        w2src(e, 0, D, j * 128, (j + 1) * 128),
                        [(dst, lambda b: b[:, 0:2048].rearrange(
                            "p (q d) -> p q d", d=512))], dep)
            fill_state["tags"][key] = last

    def fill_tag(key):
        return fill_state["tags"][key]

    t_ones = p.gc(lambda E: E.memset(ones_bf, 1.0))
    t_eps = p.gc(lambda E: E.memset(epsc[:, :], LN_EPS), t_ones)
    t_id0 = p.gc(lambda E: E.iota(ident16[:, :], [[1, 16]], channel_multiplier=-1,
                                  allow_small_or_imprecise_dtypes=True), t_ones)
    t_id = p.gc(lambda E: E.tensor_scalar(ident16[:, :], ident16[:, :], 0.0,
                                          None, ALU.is_equal), t_id0)
    t_dec0 = p.gc(lambda E: E.iota(decay[:, :], [[128, 32]], channel_multiplier=1,
                                   allow_small_or_imprecise_dtypes=True), t_id)
    t_dec = p.ac(lambda E: E.activation(decay[:, :], decay[:, :], AF.Exp,
                                        scale=-1.0 / N), t_dec0)
    t_bk = p.dma("gp", lambda E: E.dma_start(biasrow_kv[:, 0:2048], bk_d[None, :]))
    t_bv = p.dma("gp", lambda E: E.dma_start(biasrow_kv[:, 2048:4096], bv_d[None, :]))
    t_bo = p.dma("gp", lambda E: E.dma_start(biasrow_o, bo_d[None, :]))
    t_bqn = p.dma("gp", lambda E: E.dma_start(
        bq_nat[:, :], bq_d.rearrange("(a p) -> a p", p=128)))
    t_bqt = p.pe(lambda E: E.transpose(bank(7, 0, 16), bq_nat[:, :], ident16[:, :]),
                 t_bqn, t_id)
    t_bqp = p.dv(lambda E: E.tensor_copy(bq_pp[:, :], bank(7, 0, 16)), t_bqt)
    t_gw = p.dma("gp", lambda E: E.dma_start(stg[0:16, 0, :], gw_d[:, 0:1024]))
    t_gwc = p.dma("gp", lambda E: E.dma_start(cgw[:, 0:1024], stg[0:16, 0, :]), t_gw)
    t_gw2 = p.dma("gp", lambda E: E.dma_start(stg[0:16, 1, :], gw_d[:, 1024:2048]))
    t_gwc2 = p.dma("gp", lambda E: E.dma_start(cgw[:, 1024:2048], stg[0:16, 1, :]),
                   t_gw2)
    conv_state["pend"][0] = t_gwc
    conv_state["pend"][1] = t_gwc2

    run_conversions({"Wk", "Wv"})
    run_fills(["Wk", "Wv"])
    run_conversions({"Wq", "Wo"})
    run_fills(["Wq", "Wo"])
    run_conversions({"e0", "e1"})

    tags_tb = {}
    tags_tf = {}
    t_kvs = None

    with contextlib.ExitStack() as a1s:
        def al(name, shape, dt):
            return a1s.enter_context(nc.sbuf_tensor(name, shape, dt))

        WkT = al("WkT", [128, 16, 2048], BF16)
        WvT = al("WvT", [128, 16, 2048], BF16)
        TTh = al("TTh", [128, 2, 16, 256], BF16)
        xa = al("xa", [128, 2048], F32)
        tsb = al("tsb", [128, 2048], BF16)
        Kb = al("Kb", [128, 2048], BF16)
        Vb3 = al("Vb3", [128, 16, 132], BF16)
        KVacc = al("KVacc", [128, 18, 132], F32)
        sqd = tsb
        lns = al("lns", [128, 8], F32)

        tag_wk = p.dma("sy", lambda E: E.dma_start(WkT[:, :, :], tWa[1]),
                       fill_tag("Wk"))
        tag_wv = p.dma("sy", lambda E: E.dma_start(WvT[:, :, :], tWa[2]),
                       fill_tag("Wv"))

        t_vz = p.dv(lambda E: E.memset(Vb3[:, :, 128:132], 0.0))
        t_vones = p.dv(lambda E: E.memset(Vb3[:, :, 128:129], 1.0), t_vz)

        last_x_users = []
        last_t_user = None
        tt_users = [[], []]
        tt_tag = None
        prev_evac = [t_bqp]
        t_kvacc = None
        prev_d1 = None
        ln_out = {}

        for T in range(NT):
            t = T % 2
            t_x = p.dma("ac", lambda E, T=T: E.dma_start(
                xa[:, :], x_d[T * 128:(T + 1) * 128, :]), *last_x_users)
            c1, h1, prev_d1, _ = ln_stats(xa[:, :], lns, sqd,
                                          [t_x, last_t_user], prev_d1)
            i1 = p.dv(lambda E: E.tensor_scalar(tsb[:, :], xa[:, :], lns[:, 2:3],
                                                lns[:, 6:7], ALU.subtract,
                                                ALU.mult), c1, h1, last_t_user)
            last_x_users = [i1]
            last_t_user = p.dma("ac", lambda E, T=T: E.dma_start(
                Tb[T * 128:(T + 1) * 128, :], tsb[:, :]), i1)
            tags_tb[T] = last_t_user
            if T % 2 == 0 and T >= 2:
                i = T // 2 - 1
                if i + 2 <= 9:
                    run_conversions({f"e{i + 2}"})
                if i <= 7:
                    run_fills([f"e{i}"])

            if t == 1:
                half0 = T - 1
                tpar = (T // 2) % 2
                deps = [tags_tb[half0 + i] for i in range(2)] + tt_users[tpar]
                tt_users[tpar] = []
                tts = []
                for c in range(16):
                    tts.append(p.dma("sy", lambda E, c=c, h0=half0, tp2=tpar:
                                     E.dma_start(
                        TTh[:, tp2, c, :],
                        Tb[h0 * 128:h0 * 128 + 256, c * 128:(c + 1) * 128],
                        transpose=True), *deps))
                tt_tag = tts[-1]

                for tp in range(T - 1, T + 1):
                    ti = tp % 2
                    def proj(E, ti=ti, tpar=tpar):
                        ins = None
                        for c in range(16):
                            stat = TTh[:, tpar, c, ti * 128:(ti + 1) * 128]
                            for q in range(4):
                                ins = E.matmul(bank(q), stat,
                                               WkT[:, c, q * 512:(q + 1) * 512],
                                               start=(c == 0), stop=False)
                            for q in range(4):
                                ins = E.matmul(bank(4 + q), stat,
                                               WvT[:, c, q * 512:(q + 1) * 512],
                                               start=(c == 0), stop=False)
                        for q in range(4):
                            ins = E.matmul(bank(q), ones_bf[:, 0:128],
                                           biasrow_kv[:, q * 512:(q + 1) * 512],
                                           start=False, stop=True)
                        for q in range(4):
                            ins = E.matmul(bank(4 + q), ones_bf[:, 0:128],
                                           biasrow_kv[:, 2048 + q * 512:
                                                      2048 + (q + 1) * 512],
                                           start=False, stop=True)
                        return ins
                    t_proj = p.pe(proj, tt_tag, t_ones, t_bk, t_bv,
                                  tag_wk, tag_wv, *prev_evac)
                    tt_users[tpar].append(t_proj)
                    t_kb = p.ac(lambda E, tp=tp: E.activation(
                        Kb[:, :], psA[:, 0:2048], AF.Relu,
                        scale=decay[:, tp:tp + 1]), t_proj, t_dec)
                    t_vb = p.dv(lambda E: E.tensor_copy(
                        Vb3[:, :, 0:128],
                        psA[:, 2048:4096].rearrange("p (h e) -> p h e", e=128)),
                        t_proj, t_vones)
                    def kvmm(E):
                        ins = None
                        for h in range(NH):
                            ins = E.matmul(kvslot(h), Kb[:, h * 128:(h + 1) * 128],
                                           r2(Vb3[:, h:h + 1, 0:129]),
                                           start=True, stop=True)
                        return ins
                    t_kv = p.pe(kvmm, t_kb, t_vb)
                    src_a = psA[:, 0:5 * 512].rearrange(
                        "p (bk x) -> p bk x", x=512)[:, :, 0:396].rearrange(
                        "p bk (s e) -> p bk s e", e=132)[:, :, :, 0:129]
                    dst_a = KVacc.rearrange(
                        "p (bk s) e -> p bk s e", s=3)[:, 0:5, :, 0:129]
                    src_b = psA[:, 5 * 512:5 * 512 + 129]
                    dst_b = r2(KVacc[:, 15:16, 0:129])
                    if tp == 0:
                        ka = p.dv(lambda E, s=src_a, d=dst_a: E.tensor_copy(d, s),
                                  t_kv)
                        kb2 = p.dv(lambda E, s=src_b, d=dst_b: E.tensor_copy(d, s),
                                   t_kv)
                    else:
                        ka = p.dv(lambda E, s=src_a, d=dst_a: E.tensor_tensor(
                            d, s, d, ALU.add), t_kv, t_kvacc)
                        kb2 = p.dv(lambda E, s=src_b, d=dst_b: E.tensor_tensor(
                            d, s, d, ALU.add), t_kv, t_kvacc)
                    t_kvacc = kb2
                    prev_evac = [t_kb, t_vb, ka, kb2]

        t_kvs = p.dv(lambda E: E.tensor_copy(KVS[:, 0:16, :], KVacc[:, 0:16, :]),
                     t_kvacc)
        if _DBG:
            p.dma("ac", lambda E: E.dma_start(dbg["kvacc"][:, :], r2(KVacc)),
                  t_kvacc)

        p.barrier()
        p.emit_block()

    if upto == 'a1':
        t_last = p.dma("ac", lambda E: E.dma_start(
            out_d[0:128, 0:1980], r2(KVacc[:, 0:15, :])), t_kvacc)
        p.wait("sy", t_last)
        p.emit_block()
        attn_es.close()
        ges.close()
        p._stack.close()
        return nc

    with contextlib.ExitStack() as a2s:
        def al(name, shape, dt):
            return a2s.enter_context(nc.sbuf_tensor(name, shape, dt))

        wAB = al("wAB", [128, 16, 2048], BF16)
        QrT = al("QrT", [128, 16, 1024], BF16)
        tho = al("tho", [128, 2, 16, 512], BF16)
        xa = al("xa2", [128, 2048], F32)
        Oraw = al("Oraw", [128, 16, 132], F32)
        Oz = al("Oz", [128, 2, 2048], BF16)
        x1sb = al("x1sb", [128, 2048], F32)
        aow = al("aow", [128, 2048], F32)
        tfb = al("tfb", [128, 2048], BF16)
        sqd = tfb
        lns = al("lns2", [128, 8], F32)
        zz = al("zz", [128, 16], F32)

        wq_users, wo_users = [], []
        tho_users = [[], []]
        last_x_users = []
        last_x1_user = None
        last_tf_user = None
        last_oz_users = [None, None]
        qrt_war = []
        prev_q_evac = [None, None]
        oraw_users = []
        or_evac_tag = None
        ao_prev = [None, None]
        prev_d1 = None

        for g in range(G):
            tags_w = [p.dma("sy", lambda E: E.dma_start(wAB[:, :, :], tWa[0]),
                            fill_tag("Wq"), *wo_users)]
            wo_users = []
            qrt_tags = []
            for half in range(2):
                h0 = g * 1024 + half * 512
                tts = []
                for c in range(16):
                    tts.append(p.dma("sy", lambda E, c=c, h0=h0, hp=half:
                                     E.dma_start(
                        tho[:, hp, c, :], Tb[h0:h0 + 512, c * 128:(c + 1) * 128],
                        transpose=True), *tho_users[half]))
                tt_tag = tts[-1]
                tho_users[half] = []
                for dj in range(16):
                    bk_i = dj % 2
                    def qmm(E, dj=dj, bk_i=bk_i, hp=half):
                        ins = None
                        for c in range(16):
                            ins = E.matmul(bank(bk_i),
                                           wAB[:, c, dj * 128:(dj + 1) * 128],
                                           tho[:, hp, c, :], start=(c == 0),
                                           stop=(c == 15))
                        return ins
                    t_q = p.pe(qmm, tt_tag, tags_w[-1], prev_q_evac[bk_i])
                    tho_users[half].append(t_q)
                    wq_users.append(t_q)
                    t_qe = p.ac(lambda E, dj=dj, half=half, bk_i=bk_i:
                                E.activation(
                                    QrT[:, dj, half * 512:(half + 1) * 512],
                                    bank(bk_i), AF.Relu,
                                    bias=bq_pp[:, dj:dj + 1]),
                                t_q, t_bqp, *qrt_war)
                    qrt_war = []
                    prev_q_evac[bk_i] = t_qe
                    qrt_tags.append(t_qe)
            if _DBG and g == 0:
                p.dma("ac", lambda E: E.dma_start(dbg["qrt"][:, :], r2(QrT)),
                      *qrt_tags)
            if upto == 'a2q':
                t_last = p.dma("ac", lambda E: E.dma_start(
                    dbg["qrt"][:, :], r2(QrT)), *qrt_tags)
                p.wait("sy", t_last)
                p.emit_block()
                return nc
            wo_tags = [p.dma("sy", lambda E: E.dma_start(wAB[:, :, :], tWa[3]),
                             fill_tag("Wo"), *wq_users)]
            wq_users = []
            tags_ob = {}
            for t in range(GT):
                T = g * GT + t
                def omm(E, t=t):
                    ins = None
                    for h in range(NH):
                        ins = E.matmul(kvslot(h),
                                       QrT[:, h, t * 128:(t + 1) * 128],
                                       r2(KVS[:, h:h + 1, 0:129]),
                                       start=True, stop=True)
                    return ins
                t_o = p.pe(omm, qrt_tags[-1], t_kvs, or_evac_tag, *oraw_users)
                qrt_war.append(t_o)
                oraw_users = []
                ev = []
                for h in range(NH):
                    ev.append(p.ac(lambda E, h=h: E.copy(
                        r2(Oraw[:, h:h + 1, 0:129]), kvslot(h)), t_o))
                or_evac_tag = ev[-1]
                z1 = p.dv(lambda E, T=T: E.tensor_scalar(
                    zz[:, :], Oraw[:, :, 128], decay[:, T:T + 1], AT_EPS,
                    ALU.mult, ALU.add), ev[-1], t_dec)
                z2 = p.dv(lambda E: E.reciprocal(zz[:, :], zz[:, :]), z1)
                z3 = p.dv(lambda E, T=T: E.tensor_scalar(
                    zz[:, :], zz[:, :], decay[:, T:T + 1], None, ALU.mult), z2)
                par = t % 2
                oz_t = []
                for h in range(NH):
                    oz_t.append(p.dv(lambda E, h=h, par=par: E.tensor_scalar(
                        Oz[:, par, h * 128:(h + 1) * 128],
                        r2(Oraw[:, h:h + 1, 0:128]),
                        zz[:, h:h + 1], None, ALU.mult),
                        z3, last_oz_users[par]))
                oraw_users = [oz_t[-1]]
                t_obw = p.dma("ac", lambda E, T=T, par=par: E.dma_start(
                    Ob[T * 128:(T + 1) * 128, :], Oz[:, par, :]), *oz_t)
                last_oz_users[par] = t_obw
                tags_ob[t] = t_obw
            if upto == 'a2o':
                t_last = p.dma("ac", lambda E: E.dma_start(
                    dbg["tf"][0:1024, :], Ob[0:1024, :]), tags_ob[7])
                p.wait("sy", t_last)
                p.emit_block()
                return nc
            for half in range(4):
                h0 = g * 1024 + half * 256
                qpar2 = half % 2
                ots = []
                for c in range(16):
                    ots.append(p.dma("sy", lambda E, c=c, h0=h0, hp=qpar2:
                                     E.dma_start(
                        tho[:, hp, c, 0:256],
                        Ob[h0:h0 + 256, c * 128:(c + 1) * 128],
                        transpose=True),
                        tags_ob[half * 2 + 1], *tho_users[qpar2]))
                ot_tag = ots[-1]
                tho_users[qpar2] = []
                for tt_ in range(2):
                    t = half * 2 + tt_
                    T = g * GT + t
                    par = t % 2
                    b0 = par * 4
                    def aomm(E, tt_=tt_, b0=b0, hp=qpar2):
                        ins = None
                        for c in range(16):
                            stat = tho[:, hp, c, tt_ * 128:(tt_ + 1) * 128]
                            for q in range(4):
                                ins = E.matmul(bank(b0 + q), stat,
                                               wAB[:, c, q * 512:(q + 1) * 512],
                                               start=(c == 0), stop=False)
                        for q in range(4):
                            ins = E.matmul(bank(b0 + q), ones_bf[:, 0:128],
                                           biasrow_o[:, q * 512:(q + 1) * 512],
                                           start=False, stop=True)
                        return ins
                    t_ao = p.pe(aomm, ot_tag, wo_tags[-1], t_ones, t_bo,
                                ao_prev[par], or_evac_tag)
                    tho_users[qpar2].append(t_ao)
                    wo_users.append(t_ao)
                    aops = psA[:, b0 * 512:(b0 + 4) * 512]
                    if upto == 'a2w2' and g == 0 and t == 0:
                        import os
                        stage = int(os.environ.get('A2W_STAGE', '0'))
                        tags = [t_ao]
                        if stage >= 1:
                            tags.append(p.dv(lambda E: E.reduce_sum(
                                lns[:, 0:1], aops, axis=AX.X), t_ao))
                        if stage >= 2:
                            tags.append(p.ac(lambda E: E.activation(
                                sqd[:, :], aops, AF.Square,
                                accum_out=lns[:, 1:2]), t_ao))
                        if stage >= 3:
                            c1x = p.dv(lambda E: E.tensor_scalar(
                                lns[:, 2:3], lns[:, 0:1], RD, None, ALU.mult),
                                tags[1])
                            d1x = p.dv(lambda E: E.tensor_scalar(
                                lns[:, 3:4], lns[:, 1:2], RD, None, ALU.mult),
                                tags[2])
                            e1x = p.dv(lambda E: E.tensor_tensor(
                                lns[:, 4:5], lns[:, 2:3], lns[:, 2:3],
                                ALU.mult), c1x)
                            f1x = p.dv(lambda E: E.tensor_tensor(
                                lns[:, 4:5], lns[:, 3:4], lns[:, 4:5],
                                ALU.subtract), d1x, e1x)
                            tags += [c1x, d1x, e1x, f1x]
                        if stage >= 4:
                            g1x = p.ac(lambda E: E.activation(
                                lns[:, 5:6], lns[:, 4:5], AF.Sqrt,
                                bias=epsc[:, :]), f1x, t_eps)
                            h1x = p.dv(lambda E: E.reciprocal(
                                lns[:, 6:7], lns[:, 5:6]), g1x)
                            tags += [g1x, h1x]
                        if stage >= 5:
                            tags.append(p.dv(lambda E: E.tensor_scalar(
                                aow[:, :], aops, lns[:, 2:3], lns[:, 6:7],
                                ALU.subtract, ALU.mult), c1x, h1x))
                        tcp = p.dv(lambda E, aops=aops: E.tensor_copy(
                            aow[:, :], aops), *tags)
                        t_last = p.dma("ac", lambda E: E.dma_start(
                            dbg["x1"][0:128, :], aow[:, :]), tcp)
                        p.wait("sy", t_last)
                        p.emit_block()
                        return nc
                    tcp = p.dv(lambda E, aops=aops: E.tensor_copy(
                        aow[:, :], aops), t_ao)
                    ao_prev[par] = tcp
                    c1, h1, prev_d1, _ = ln_stats(aow[:, :], lns, sqd,
                                                  [tcp, last_tf_user], prev_d1)
                    h2 = p.dv(lambda E: E.tensor_scalar(
                        aow[:, :], aow[:, :], lns[:, 2:3], lns[:, 6:7],
                        ALU.subtract, ALU.mult), c1, h1)
                    if upto == 'a2w' and g == 0 and t == 0:
                        t_last = p.dma("ac", lambda E: E.dma_start(
                            dbg["x1"][0:128, :], aow[:, :]), h2)
                        p.wait("sy", t_last)
                        p.emit_block()
                        return nc
                    t_x = p.dma("ac", lambda E, T=T: E.dma_start(
                        xa[:, :], x_d[T * 128:(T + 1) * 128, :]), *last_x_users)
                    i1 = p.dv(lambda E: E.tensor_tensor(x1sb[:, :], aow[:, :],
                                                        xa[:, :], ALU.add),
                              h2, t_x, last_x1_user)
                    last_x_users = [i1]
                    t_x1w = p.dma("ac", lambda E, T=T: E.dma_start(
                        X1[T * 128:(T + 1) * 128, :], x1sb[:, :]), i1)
                    if _DBG:
                        p.dma("ac", lambda E, T=T: E.dma_start(
                            dbg["x1"][T * 128:(T + 1) * 128, :], x1sb[:, :]), i1)
                    c1, h1, prev_d1, _ = ln_stats(x1sb[:, :], lns, sqd, [i1],
                                                  prev_d1)
                    j1 = p.dv(lambda E: E.tensor_scalar(
                        tfb[:, :], x1sb[:, :], lns[:, 2:3], lns[:, 6:7],
                        ALU.subtract, ALU.mult), c1, h1, last_tf_user)
                    last_x1_user = j1
                    t_tfw = p.dma("ac", lambda E, T=T: E.dma_start(
                        Tf[T * 128:(T + 1) * 128, :], tfb[:, :]), j1)
                    if _DBG:
                        p.dma("ac", lambda E, T=T: E.dma_start(
                            dbg["tf"][T * 128:(T + 1) * 128, :], tfb[:, :]), j1)
                    last_tf_user = t_tfw
                    tags_tf[T] = t_tfw
            run_conversions({f"e{10 + 2 * g}", f"e{11 + 2 * g}"})
            run_fills([f"e{8 + 2 * g}", f"e{9 + 2 * g}"])
        p.barrier()
        p.emit_block()
    attn_es.close()

    if upto == 'a2':
        t_last = p.dma("ac", lambda E: E.dma_start(
            out_d[:, :], X1[:, :]), tags_tf[NT - 1])
        p.wait("sy", t_last)
        p.emit_block()
        ges.close()
        p._stack.close()
        return nc

    with contextlib.ExitStack() as f_s:
        def al(name, shape, dt):
            return f_s.enter_context(nc.sbuf_tensor(name, shape, dt))

        TTf = al("TTf", [128, 16, 1024], BF16)
        Ysb = al("Ysb", [128, 8, 2048], F32)
        Hsb = al("Hsb", [128, 11, 1024], BF16)
        w13 = [[al(f"w13_{w}_{i}", [128, 16, 256], BF16) for i in range(2)]
               for w in range(2)]
        w2q = [al(f"w2q_{i}", [128, 11, 512], BF16) for i in range(2)]
        silu = al("silu", [128, 1, 1024], BF16)
        weF = al("weF", [128, 128], F32)
        glog = al("glog", [128, 8, 16], F32)
        gm = al("gm", [128, 4], F32)
        gmk = al("gmk", [128, 3, 16], F32)
        gateT = al("gateT", [128, 16, 16], BF16)
        x1a = al("x1a", [128, 2048], F32)
        outsb = al("outsb", [128, 2048], F32)
        sqd = outsb
        lns = al("lns3", [128, 8], F32)

        JBS = [(0, 2), (2, 2), (4, 2), (6, 2), (8, 2), (10, 1)]

        p.wait("sy", conv_tag("e17"))
        p.wait("ac", conv_tag("e17"))
        w13_users = [[], []]
        w2_users = [[], []]
        h_users = []
        y_evac = [None, None]
        h_evac = {}
        ttf_users = []
        gate_users = []
        y_users = []
        last_x1_users = []
        last_out_user = None
        out_tags = []
        sil_users = [[], []]
        prev_d1 = None
        prev_wf = None

        for g in range(G):
            g0 = g * 1024
            ttfs = []
            for c in range(16):
                ttfs.append(p.dma("sy", lambda E, c=c, g0=g0: E.dma_start(
                    TTf[:, c, :], Tf[g0:g0 + 1024, c * 128:(c + 1) * 128],
                    transpose=True), tags_tf[g * GT + 7], *ttf_users))
            ttf_users = []
            ttf_tag = ttfs[-1]
            gts = []
            for c in range(16):
                gts.append(p.dma("sy", lambda E, c=c: E.dma_start(
                    gateT[:, c, :], cgw[:, c * 128:(c + 1) * 128],
                    transpose=True), t_gwc2, *gate_users))
            gate_users = []
            we_tags = []
            prev_g = None
            for t in range(GT):
                def gmm(E, t=t):
                    ins = None
                    for c in range(16):
                        ins = E.matmul(bank(6, 0, 16),
                                       TTf[:, c, t * 128:(t + 1) * 128],
                                       gateT[:, c, :], start=(c == 0),
                                       stop=(c == 15))
                    return ins
                t_g = p.pe(gmm, ttf_tag, gts[-1], prev_g)
                gate_users.append(t_g)
                ttf_users.append(t_g)
                ge = p.ac(lambda E, t=t: E.copy(glog[:, t, :], bank(6, 0, 16)),
                          t_g)
                prev_g = ge
                m0 = p.dv(lambda E, t=t: E.reduce_max(gm[:, 0:1], glog[:, t, :],
                                                      axis=AX.X), ge)
                m1 = p.dv(lambda E: E.tensor_scalar(gm[:, 1:2], gm[:, 0:1], -1.0,
                                                    None, ALU.mult), m0)
                ex = p.ac(lambda E, t=t: E.activation(gmk[:, 0, :], glog[:, t, :],
                                                      AF.Exp, bias=gm[:, 1:2]),
                          m1, prev_wf)
                s0 = p.dv(lambda E: E.reduce_sum(gm[:, 2:3], gmk[:, 0, :],
                                                 axis=AX.X), ex)
                s1 = p.dv(lambda E: E.reciprocal(gm[:, 2:3], gm[:, 2:3]), s0)
                pr = p.dv(lambda E: E.tensor_scalar(gmk[:, 0, :], gmk[:, 0, :],
                                                    gm[:, 2:3], None, ALU.mult),
                          s1)
                r1 = p.dv(lambda E: E.reduce_max(gm[:, 0:1], gmk[:, 0, :],
                                                 axis=AX.X), pr)
                k1 = p.dv(lambda E: E.tensor_scalar(gmk[:, 1, :], gmk[:, 0, :],
                                                    gm[:, 0:1], None, ALU.is_ge),
                          r1)
                k2 = p.dv(lambda E: E.tensor_scalar(gmk[:, 2, :], gmk[:, 1, :],
                                                    -1.0, 1.0, ALU.mult, ALU.add),
                          k1)
                p2 = p.dv(lambda E: E.tensor_tensor(gmk[:, 2, :], gmk[:, 0, :],
                                                    gmk[:, 2, :], ALU.mult), k2)
                r2_ = p.dv(lambda E: E.reduce_max(gm[:, 1:2], gmk[:, 2, :],
                                                  axis=AX.X), p2)
                k3 = p.dv(lambda E: E.tensor_scalar(gmk[:, 2, :], gmk[:, 2, :],
                                                    gm[:, 1:2], None, ALU.is_ge),
                          r2_)
                k4 = p.dv(lambda E: E.tensor_tensor(gmk[:, 1, :], gmk[:, 1, :],
                                                    gmk[:, 2, :], ALU.add), k3)
                wf = p.dv(lambda E, t=t: E.tensor_tensor(
                    weF[:, t * 16:(t + 1) * 16], gmk[:, 0, :], gmk[:, 1, :],
                    ALU.mult), k4, pr)
                we_tags.append(wf)
                prev_wf = wf
            if _DBG and g < 2:
                p.dma("ac", lambda E, g=g: E.dma_start(
                    dbg["we"][:, g * 128:(g + 1) * 128], weF[:, :]), *we_tags)

            for e in range(18):
                if g == 0 and e in (1, 3):
                    run_fills([{1: "e16", 3: "e17"}[e]])
                w13_tags = {}

                def load_w13(bi, e=e):
                    par = bi % 2
                    deps = [e_fill_tag(e)] + w13_users[par]
                    w13_users[par] = []
                    tag = p.dma("sy", lambda E, bi=bi, par=par, e=e:
                                E.dma_start(w13[0][par][:, :, :],
                                            tw13[e, 0, bi]), *deps)
                    tag = p.dma("sy", lambda E, bi=bi, par=par, e=e:
                                E.dma_start(w13[1][par][:, :, :],
                                            tw13[e, 1, bi]), *deps)
                    w13_tags[bi] = tag

                load_w13(0)
                load_w13(1)
                for bi, (j0, njs) in enumerate(JBS):
                    par = bi % 2
                    if bi >= 2:
                        pass
                    for jj in range(njs):
                        j = j0 + jj
                        jpar = j % 2
                        b0 = jpar * 4
                        def hmm(E, jj=jj, par=par, b0=b0):
                            ins = None
                            for c in range(16):
                                s1_ = w13[0][par][:, c, jj * 128:(jj + 1) * 128]
                                s3_ = w13[1][par][:, c, jj * 128:(jj + 1) * 128]
                                ins = E.matmul(bank(b0 + 0), s1_,
                                               TTf[:, c, 0:512],
                                               start=(c == 0), stop=(c == 15))
                                ins = E.matmul(bank(b0 + 1), s1_,
                                               TTf[:, c, 512:1024],
                                               start=(c == 0), stop=(c == 15))
                                ins = E.matmul(bank(b0 + 2), s3_,
                                               TTf[:, c, 0:512],
                                               start=(c == 0), stop=(c == 15))
                                ins = E.matmul(bank(b0 + 3), s3_,
                                               TTf[:, c, 512:1024],
                                               start=(c == 0), stop=(c == 15))
                            return ins
                        extra = [we_tags[-1]] if (e == 0 and bi == 0) else []
                        t_h = p.pe(hmm, ttf_tag, w13_tags[bi],
                                   h_evac.get(j - 2), h_evac.get(j - 1),
                                   *(y_evac if (e > 0 or g > 0) else []),
                                   *extra, *h_users)
                        h_users = []
                        w13_users[par].append(t_h)
                        ttf_users.append(t_h)
                        t_sil = p.ac(lambda E, b0=b0: E.activation(
                            silu[:, 0, :], psA[:, b0 * 512:(b0 + 2) * 512],
                            AF.Silu), t_h, *sil_users[0])
                        sil_users[0] = []
                        t_he = p.dv(lambda E, j=j, b0=b0:
                                    E.tensor_tensor(
                                        Hsb[:, j, :], silu[:, 0, :],
                                        psA[:, (b0 + 2) * 512:(b0 + 4) * 512],
                                        ALU.mult), t_sil, h_evac.get(j))
                        sil_users[0].append(t_he)
                        h_evac[j] = t_he
                    if bi + 2 < len(JBS):
                        load_w13(bi + 2)
                if _DBG and g == 1 and e == 0:
                    p.dma("ac", lambda E: E.dma_start(
                        dbg["h"][:, :], r2(Hsb)), h_evac[10], h_evac[9])
                    p.dma("ac", lambda E: E.dma_start(
                        dbg["ttf"][:, :], r2(TTf)), h_evac[10])
                for dq in range(4):
                    qpar = (e * 4 + dq) % 2
                    deps = [e_fill_tag(e)] + w2_users[qpar]
                    w2_users[qpar] = []
                    w2tag = p.dma("sy", lambda E, dq=dq, qpar=qpar, e=e:
                                  E.dma_start(w2q[qpar][:, :, :],
                                              tw2q[e, dq]), *deps)
                    for t in range(GT):
                        ypar = t % 2
                        def ymm(E, t=t, qpar=qpar, ypar=ypar):
                            ins = None
                            for j in range(NJ):
                                ins = E.matmul(bank(ypar),
                                               Hsb[:, j, t * 128:(t + 1) * 128],
                                               w2q[qpar][:, j, :],
                                               start=(j == 0),
                                               stop=(j == NJ - 1))
                            return ins
                        t_y = p.pe(ymm, w2tag, h_evac[NJ - 1], h_evac[NJ - 2],
                                   y_evac[ypar])
                        w2_users[qpar].append(t_y)
                        h_users.append(t_y)
                        ydst = Ysb[:, t, dq * 512:(dq + 1) * 512]
                        wecol = (weF[:, t * 16 + e:t * 16 + e + 1]
                                 if e < NEXP else None)
                        if e == 0:
                            t_ye = p.dv(lambda E, ydst=ydst, wecol=wecol,
                                        ypar=ypar: E.tensor_scalar(
                                            ydst, bank(ypar), wecol, None,
                                            ALU.mult),
                                        t_y, we_tags[t], *y_users)
                        elif e < NEXP:
                            t_ye = p.dv(lambda E, ydst=ydst, wecol=wecol,
                                        ypar=ypar: E.scalar_tensor_tensor(
                                            ydst, bank(ypar), wecol, ydst,
                                            ALU.mult, ALU.add),
                                        t_y, we_tags[t])
                        else:
                            t_ye = p.dv(lambda E, ydst=ydst, ypar=ypar:
                                        E.tensor_tensor(ydst, bank(ypar), ydst,
                                                        ALU.add), t_y)
                        y_evac[ypar] = t_ye
                y_users = []
            if _DBG and g < 2:
                p.dma("ac", lambda E, g=g: E.dma_start(
                    dbg["y"][:, g * 16384:(g + 1) * 16384], r2(Ysb)),
                    y_evac[0], y_evac[1])
            for t in range(GT):
                T = g * GT + t
                ysl = Ysb[:, t, :]
                c1, h1, prev_d1, btag = ln_stats(ysl, lns, sqd,
                                                 [y_evac[0], y_evac[1],
                                                  last_out_user],
                                                 prev_d1)
                y_users.append(btag)
                t_x1 = p.dma("ac", lambda E, T=T: E.dma_start(
                    x1a[:, :], X1[T * 128:(T + 1) * 128, :]), *last_x1_users)
                h2 = p.dv(lambda E, ysl=ysl: E.tensor_scalar(
                    outsb[:, :], ysl, lns[:, 2:3], lns[:, 6:7],
                    ALU.subtract, ALU.mult), c1, h1, last_out_user)
                y_users.append(h2)
                i1 = p.dv(lambda E: E.tensor_tensor(outsb[:, :], outsb[:, :],
                                                    x1a[:, :], ALU.add),
                          h2, t_x1)
                y_users.append(i1)
                last_x1_users = [i1]
                t_ow = p.dma("ac", lambda E, T=T: E.dma_start(
                    out_d[T * 128:(T + 1) * 128, :], outsb[:, :]), i1)
                last_out_user = t_ow
                out_tags.append(t_ow)
        p.wait("sy", out_tags[-1])
        p.wait("ac", out_tags[-1])
        p.emit_block()

    ges.close()
    p._stack.close()
    return nc


_IN_MAP = {
    "L0_attn_Wq": "Wq", "L0_attn_Wk": "Wk", "L0_attn_Wv": "Wv",
    "L0_attn_Wo": "Wo", "L0_attn_bq": "bq", "L0_attn_bk": "bk",
    "L0_attn_bv": "bv", "L0_attn_bo": "bo", "L0_ffn_gate_w": "gw",
    "L0_ffn_w1": "w1", "L0_ffn_w2": "w2", "L0_ffn_w3": "w3",
    "L0_ffn_sw1": "sw1", "L0_ffn_sw2": "sw2", "L0_ffn_sw3": "sw3",
}


def kernel(**inputs):
    x = np.ascontiguousarray(np.asarray(inputs["L0_attn_x"], dtype=np.float32))
    common = {v: np.ascontiguousarray(np.asarray(inputs[k], dtype=np.float32))
              for k, v in _IN_MAP.items()}
    nc = build()
    in_maps = []
    for i in range(8):
        m = dict(common)
        m["x"] = x[i]
        in_maps.append(m)
    res = run_bass_kernel_spmd(nc, in_maps, core_ids=list(range(8)))
    return np.stack([res.results[i]["out"] for i in range(8)], axis=0)


# version 05 (final)
# speedup vs baseline: 1.1499x; 1.1499x over previous
import contextlib
import numpy as np

import concourse.bass as bass
import concourse.mybir as mybir
from concourse.bass_utils import run_bass_kernel_spmd

F32 = mybir.dt.float32
BF16 = mybir.dt.bfloat16
AF = mybir.ActivationFunctionType
ALU = mybir.AluOpType
AX = mybir.AxisListType

N = 4096
D = 2048
NH = 16
NEXP = 16
DI = 1408
NJ = 11
NT = 32
G = 4
GT = 8
LN_EPS = 1e-5
AT_EPS = 1e-6
RD = 1.0 / D

ENGINES = ("sy", "gp", "pe", "dv", "ac")
_DBG = False


class Prog:

    def __init__(self, nc):
        self.nc = nc
        self.cnt = {}
        self.handles = {}
        self.waited = {e: {} for e in ENGINES}
        self.items = {e: [] for e in ENGINES}
        self._stack = contextlib.ExitStack()

    def sem(self, name):
        self.handles[name] = self._stack.enter_context(self.nc.semaphore(name))
        self.cnt[name] = 0

    def wait(self, eng, dep):
        if dep is None:
            return
        s, v = dep
        if self.waited[eng].get(s, 0) >= v:
            return
        self.waited[eng][s] = v
        h = self.handles[s]
        self.items[eng].append(("w", h, v))

    def step(self, eng, fn, waits=(), inc=None, k=1):
        for d in waits:
            self.wait(eng, d)
        tag = None
        if inc is not None:
            self.cnt[inc] += k
            tag = (inc, self.cnt[inc])
            self.items[eng].append(("i", fn, self.handles[inc], k))
        else:
            self.items[eng].append(("i", fn, None, 0))
        return tag

    def dv(self, fn, *waits):
        return self.step("dv", fn, waits, inc="DV")

    def ac(self, fn, *waits):
        return self.step("ac", fn, waits, inc="AC")

    def gc(self, fn, *waits):
        return self.step("gp", fn, waits, inc="GC")

    def pe(self, fn, *waits, tag=True):
        return self.step("pe", fn, waits, inc="PE" if tag else None)

    def dma(self, q, fn, *waits):
        semname = {"sy": "SY", "gp": "GP", "ac": "AD"}[q]
        return self.step(q, fn, waits, inc=semname, k=16)

    def barrier(self, engines=("pe", "dv", "ac", "sy")):
        snap = [("PE", self.cnt["PE"]), ("DV", self.cnt["DV"]),
                ("AC", self.cnt["AC"]), ("SY", self.cnt["SY"]),
                ("AD", self.cnt["AD"])]
        for e in engines:
            for d in snap:
                self.wait(e, d)

    def emit_block(self):
        items = self.items
        self.items = {e: [] for e in ENGINES}
        with self.nc.Block() as block:
            def runner(eng_key):
                def run(E):
                    for it in items[eng_key]:
                        if it[0] == "w":
                            E.wait_ge(it[1], it[2])
                        else:
                            ins = it[1](E)
                            if it[2] is not None:
                                ins.then_inc(it[2], it[3])
                return run
            block.sync(runner("sy"))
            block.gpsimd(runner("gp"))
            block.tensor(runner("pe"))
            block.vector(runner("dv"))
            block.scalar(runner("ac"))


def r2(ap3):
    return ap3.rearrange("p a b -> p (a b)")


def build(nc=None, upto='all'):
    nc = nc or bass.Bass("TRN2", target_bir_lowering=False)
    p = Prog(nc)
    for s in ("SY", "GP", "AD", "PE", "DV", "AC", "GC"):
        p.sem(s)

    x_d = nc.dram_tensor("x", [N, D], F32, kind="ExternalInput")
    Wq_d = nc.dram_tensor("Wq", [D, D], F32, kind="ExternalInput")
    Wk_d = nc.dram_tensor("Wk", [D, D], F32, kind="ExternalInput")
    Wv_d = nc.dram_tensor("Wv", [D, D], F32, kind="ExternalInput")
    Wo_d = nc.dram_tensor("Wo", [D, D], F32, kind="ExternalInput")
    bq_d = nc.dram_tensor("bq", [D], F32, kind="ExternalInput")
    bk_d = nc.dram_tensor("bk", [D], F32, kind="ExternalInput")
    bv_d = nc.dram_tensor("bv", [D], F32, kind="ExternalInput")
    bo_d = nc.dram_tensor("bo", [D], F32, kind="ExternalInput")
    gw_d = nc.dram_tensor("gw", [NEXP, D], F32, kind="ExternalInput")
    w1_d = nc.dram_tensor("w1", [NEXP, DI, D], F32, kind="ExternalInput")
    w2_d = nc.dram_tensor("w2", [NEXP, D, DI], F32, kind="ExternalInput")
    w3_d = nc.dram_tensor("w3", [NEXP, DI, D], F32, kind="ExternalInput")
    sw1_d = nc.dram_tensor("sw1", [2 * DI, D], F32, kind="ExternalInput")
    sw2_d = nc.dram_tensor("sw2", [D, 2 * DI], F32, kind="ExternalInput")
    sw3_d = nc.dram_tensor("sw3", [2 * DI, D], F32, kind="ExternalInput")
    out_d = nc.dram_tensor("out", [N, D], F32, kind="ExternalOutput")

    cWq = nc.dram_tensor("cWq", [D, D], BF16)
    cWk = nc.dram_tensor("cWk", [D, D], BF16)
    cWv = nc.dram_tensor("cWv", [D, D], BF16)
    cWo = nc.dram_tensor("cWo", [D, D], BF16)
    cgw = nc.dram_tensor("cgw", [NEXP, D], BF16)
    cw1 = nc.dram_tensor("cw1", [NEXP, DI, D], BF16)
    cw2 = nc.dram_tensor("cw2", [NEXP, D, DI], BF16)
    cw3 = nc.dram_tensor("cw3", [NEXP, DI, D], BF16)
    csw1 = nc.dram_tensor("csw1", [2 * DI, D], BF16)
    csw2 = nc.dram_tensor("csw2", [D, 2 * DI], BF16)
    csw3 = nc.dram_tensor("csw3", [2 * DI, D], BF16)
    tw13 = nc.dram_tensor("tw13", [18, 2, 6, 128, 16, 256], BF16)
    tw2q = nc.dram_tensor("tw2q", [18, 4, 128, 11, 512], BF16)
    tWa = nc.dram_tensor("tWa", [4, 128, 16, 2048], BF16)
    Tb = nc.dram_tensor("Tb", [N, D], BF16)
    Ob = nc.dram_tensor("Ob", [N, D], BF16)
    Tf = nc.dram_tensor("Tf", [N, D], BF16)
    X1 = nc.dram_tensor("X1", [N, D], F32)

    dbg = {}
    if _DBG:
        dbg["kvacc"] = nc.dram_tensor("d_kvacc", [128, 18 * 132], F32,
                                      kind="ExternalOutput")
        dbg["x1"] = nc.dram_tensor("d_x1", [N, D], F32, kind="ExternalOutput")
        dbg["tf"] = nc.dram_tensor("d_tf", [N, D], BF16, kind="ExternalOutput")
        dbg["we"] = nc.dram_tensor("d_we", [128, 256], F32,
                                   kind="ExternalOutput")
        dbg["qrt"] = nc.dram_tensor("d_qrt", [128, 16 * 1024], BF16,
                                    kind="ExternalOutput")
        dbg["y"] = nc.dram_tensor("d_y", [128, 16 * 2048], F32,
                                  kind="ExternalOutput")
        dbg["h"] = nc.dram_tensor("d_h", [128, 11 * 1024], BF16,
                                  kind="ExternalOutput")
        dbg["ttf"] = nc.dram_tensor("d_ttf", [128, 16 * 1024], BF16,
                                    kind="ExternalOutput")

    ges = contextlib.ExitStack()

    def galloc(name, shape, dt):
        return ges.enter_context(nc.sbuf_tensor(name, shape, dt))

    stg2 = galloc("stg2", [128, 3, 2048], BF16)
    stg = galloc("stg", [128, 2, 1024], BF16)
    epsc = galloc("epsc", [128, 1], F32)
    psA = ges.enter_context(nc.psum_tensor("psA", [128, 4096], F32))

    attn_es = contextlib.ExitStack()

    def aalloc(name, shape, dt):
        return attn_es.enter_context(nc.sbuf_tensor(name, shape, dt))

    rows_bf = aalloc("rows_bf", [1, 4096 + 2048 + 128], BF16)
    biasrow_kv = rows_bf[:, 0:4096]
    biasrow_o = rows_bf[:, 4096:6144]
    ones_bf = rows_bf[:, 6144:6272]
    bq_nat = aalloc("bq_nat", [16, 128], F32)
    bq_pp = aalloc("bq_pp", [128, 16], F32)
    ident16 = aalloc("ident16", [16, 16], F32)
    decay = aalloc("decay", [128, 32], F32)
    KVS = aalloc("KVS", [128, 18, 132], BF16)

    def bank(i, off=0, n=512):
        return psA[:, i * 512 + off:i * 512 + off + n]

    def kvslot(h):
        base = (h // 3) * 512 + (h % 3) * 132
        return psA[:, base:base + 129]

    def ln_stats(src, lns, sqd, deps, b_war):
        a = p.dv(lambda E: E.reduce_sum(lns[:, 0:1], src, axis=AX.X), *deps)
        b = p.ac(lambda E: E.activation(sqd[:, :], src, AF.Square,
                                        accum_out=lns[:, 1:2]), *deps, b_war)
        c1 = p.dv(lambda E: E.tensor_scalar(lns[:, 2:3], lns[:, 0:1], RD,
                                            None, ALU.mult), a)
        d1 = p.dv(lambda E: E.tensor_scalar(lns[:, 3:4], lns[:, 1:2], RD,
                                            None, ALU.mult), b)
        e1 = p.dv(lambda E: E.tensor_tensor(lns[:, 4:5], lns[:, 2:3],
                                            lns[:, 2:3], ALU.mult), c1)
        f1 = p.dv(lambda E: E.tensor_tensor(lns[:, 4:5], lns[:, 3:4],
                                            lns[:, 4:5], ALU.subtract), d1, e1)
        g1 = p.ac(lambda E: E.activation(lns[:, 5:6], lns[:, 4:5], AF.Sqrt,
                                         bias=epsc[:, :]), f1, t_eps)
        h1 = p.dv(lambda E: E.reciprocal(lns[:, 6:7], lns[:, 5:6]), g1)
        return c1, h1, d1, b

    def conv(dst_dram, src_dram, rows, cols):
        jobs = []
        for a in range(rows // 128):
            for c0 in range(0, cols, 1024):
                c1 = min(c0 + 1024, cols)
                jobs.append((src_dram[a * 128:(a + 1) * 128, c0:c1],
                             dst_dram[a * 128:(a + 1) * 128, c0:c1], c1 - c0))
        return jobs

    conv_seq = [("Wk", conv(cWk, Wk_d, D, D)), ("Wv", conv(cWv, Wv_d, D, D)),
                ("Wq", conv(cWq, Wq_d, D, D)), ("Wo", conv(cWo, Wo_d, D, D))]
    for e in range(NEXP):
        conv_seq.append((f"e{e}", conv(cw1[e], w1_d[e], DI, D)
                         + conv(cw3[e], w3_d[e], DI, D)
                         + conv(cw2[e], w2_d[e], D, DI)))
    conv_seq.append(("e16", conv(csw1[0:DI], sw1_d[0:DI], DI, D)
                     + conv(csw3[0:DI], sw3_d[0:DI], DI, D)))
    conv_seq.append(("e17", conv(csw1[DI:2 * DI], sw1_d[DI:2 * DI], DI, D)
                     + conv(csw3[DI:2 * DI], sw3_d[DI:2 * DI], DI, D)
                     + conv(csw2, sw2_d, D, 2 * DI)))

    conv_state = {"i": 0, "pend": [None, None], "tags": {}}

    def run_conversions(keys):
        for key, jobs in conv_seq:
            if key not in keys:
                continue
            last = None
            for (src, dst, cols) in jobs:
                i = conv_state["i"]
                buf = stg[:, i % 2, 0:cols]
                rd = p.dma("gp", lambda E, b=buf, s=src: E.dma_start(b, s),
                           conv_state["pend"][i % 2])
                wr = p.dma("gp", lambda E, b=buf, d=dst: E.dma_start(d, b), rd)
                conv_state["pend"][i % 2] = wr
                conv_state["i"] += 1
                last = wr
            conv_state["tags"][key] = last

    def conv_tag(key):
        return conv_state["tags"][key]

    def w1src(e, r0, r1, c0, c1):
        if e < NEXP:
            return cw1[e][r0:r1, c0:c1]
        return csw1[(e - NEXP) * DI + r0:(e - NEXP) * DI + r1, c0:c1]

    def w3src(e, r0, r1, c0, c1):
        if e < NEXP:
            return cw3[e][r0:r1, c0:c1]
        return csw3[(e - NEXP) * DI + r0:(e - NEXP) * DI + r1, c0:c1]

    def w2src(e, r0, r1, c0, c1):
        if e < NEXP:
            return cw2[e][r0:r1, c0:c1]
        return csw2[r0:r1, (e - NEXP) * DI + c0:(e - NEXP) * DI + c1]

    def e_fill_tag(e):
        return fill_tag(f"e{e}")

    fill_state = {"i": 0, "pend": [None, None, None], "tags": {}}

    def _fill_job(tr_src, writes, dep):
        i = fill_state["i"]
        fill_state["i"] += 1
        buf = stg2[:, i % 3, :]
        rd = p.dma("sy", lambda E, b=buf, s=tr_src: E.dma_start(
            b[:, 0:s.shape[0]], s, transpose=True),
            dep, fill_state["pend"][i % 3])
        last = None
        for (dst, sl) in writes:
            last = p.dma("gp", lambda E, b=buf, d=dst, sl=sl: E.dma_start(
                d, sl(b)), rd)
        fill_state["pend"][i % 3] = last
        return last

    def run_fills(keys):
        for key in keys:
            dep = conv_tag(key)
            last = None
            if key in ("Wq", "Wk", "Wv", "Wo"):
                w = {"Wq": 0, "Wk": 1, "Wv": 2, "Wo": 3}[key]
                src = {"Wq": cWq, "Wk": cWk, "Wv": cWv, "Wo": cWo}[key]
                for c in range(16):
                    last = _fill_job(
                        src[:, c * 128:(c + 1) * 128],
                        [(tWa[w][:, c, :], lambda b: b[:, 0:2048])], dep)
            else:
                e = int(key[1:])
                if e == 16 and "e17" in conv_state["tags"]:
                    dep = conv_tag("e17")
                for w, srcfn in ((0, w1src), (1, w3src)):
                    for c in range(16):
                        dst = tw13[e, w].rearrange(
                            "b p c j -> p c b j")[:, c, :, :]
                        last = _fill_job(
                            srcfn(e, 0, DI, c * 128, (c + 1) * 128),
                            [(dst, lambda b: b[:, 0:1536].rearrange(
                                "p (b j) -> p b j", j=256))], dep)
                for j in range(NJ):
                    dst = tw2q[e].rearrange("q p j d -> p j q d")[:, j, :, :]
                    last = _fill_job(
                        w2src(e, 0, D, j * 128, (j + 1) * 128),
                        [(dst, lambda b: b[:, 0:2048].rearrange(
                            "p (q d) -> p q d", d=512))], dep)
            fill_state["tags"][key] = last

    def fill_tag(key):
        return fill_state["tags"][key]

    t_ones = p.gc(lambda E: E.memset(ones_bf, 1.0))
    t_eps = p.gc(lambda E: E.memset(epsc[:, :], LN_EPS), t_ones)
    t_id0 = p.gc(lambda E: E.iota(ident16[:, :], [[1, 16]], channel_multiplier=-1,
                                  allow_small_or_imprecise_dtypes=True), t_ones)
    t_id = p.gc(lambda E: E.tensor_scalar(ident16[:, :], ident16[:, :], 0.0,
                                          None, ALU.is_equal), t_id0)
    t_dec0 = p.gc(lambda E: E.iota(decay[:, :], [[128, 32]], channel_multiplier=1,
                                   allow_small_or_imprecise_dtypes=True), t_id)
    t_dec = p.ac(lambda E: E.activation(decay[:, :], decay[:, :], AF.Exp,
                                        scale=-1.0 / N), t_dec0)
    t_bk = p.dma("gp", lambda E: E.dma_start(biasrow_kv[:, 0:2048], bk_d[None, :]))
    t_bv = p.dma("gp", lambda E: E.dma_start(biasrow_kv[:, 2048:4096], bv_d[None, :]))
    t_bo = p.dma("gp", lambda E: E.dma_start(biasrow_o, bo_d[None, :]))
    t_bqn = p.dma("gp", lambda E: E.dma_start(
        bq_nat[:, :], bq_d.rearrange("(a p) -> a p", p=128)))
    t_bqt = p.pe(lambda E: E.transpose(bank(7, 0, 16), bq_nat[:, :], ident16[:, :]),
                 t_bqn, t_id)
    t_bqp = p.dv(lambda E: E.tensor_copy(bq_pp[:, :], bank(7, 0, 16)), t_bqt)
    t_gw = p.dma("gp", lambda E: E.dma_start(stg[0:16, 0, :], gw_d[:, 0:1024]))
    t_gwc = p.dma("gp", lambda E: E.dma_start(cgw[:, 0:1024], stg[0:16, 0, :]), t_gw)
    t_gw2 = p.dma("gp", lambda E: E.dma_start(stg[0:16, 1, :], gw_d[:, 1024:2048]))
    t_gwc2 = p.dma("gp", lambda E: E.dma_start(cgw[:, 1024:2048], stg[0:16, 1, :]),
                   t_gw2)
    conv_state["pend"][0] = t_gwc
    conv_state["pend"][1] = t_gwc2

    run_conversions({"Wk", "Wv"})
    run_fills(["Wk", "Wv"])
    run_conversions({"Wq", "Wo"})
    run_fills(["Wq", "Wo"])
    run_conversions({"e0", "e1"})

    tags_tb = {}
    tags_tf = {}
    t_kvs = None

    with contextlib.ExitStack() as a1s:
        def al(name, shape, dt):
            return a1s.enter_context(nc.sbuf_tensor(name, shape, dt))

        WkT = al("WkT", [128, 16, 2048], BF16)
        WvT = al("WvT", [128, 16, 2048], BF16)
        TTh = al("TTh", [128, 1, 16, 256], BF16)
        xa = al("xa", [128, 2, 2048], F32)
        tsb = al("tsb", [128, 2048], BF16)
        Kb = al("Kb", [128, 2048], BF16)
        Vb3 = al("Vb3", [128, 16, 132], BF16)
        KVacc = al("KVacc", [128, 18, 132], F32)
        sqd = tsb
        lns = al("lns", [128, 8], F32)

        tag_wk = p.dma("sy", lambda E: E.dma_start(WkT[:, :, :], tWa[1]),
                       fill_tag("Wk"))
        tag_wv = p.dma("sy", lambda E: E.dma_start(WvT[:, :, :], tWa[2]),
                       fill_tag("Wv"))

        t_vz = p.dv(lambda E: E.memset(Vb3[:, :, 128:132], 0.0))
        t_vones = p.dv(lambda E: E.memset(Vb3[:, :, 128:129], 1.0), t_vz)

        last_x_users = [[], []]
        last_t_user = None
        tt_users = [[], []]
        tt_tag = None
        prev_evac = [t_bqp]
        t_kvacc = None
        prev_d1 = None
        ln_out = {}

        for T in range(NT):
            t = T % 2
            xp = T % 2
            t_x = p.dma("ac", lambda E, T=T, xp=xp: E.dma_start(
                xa[:, xp, :], x_d[T * 128:(T + 1) * 128, :]), *last_x_users[xp])
            c1, h1, prev_d1, _ = ln_stats(xa[:, xp, :], lns, sqd,
                                          [t_x, last_t_user], prev_d1)
            i1 = p.dv(lambda E, xp=xp: E.tensor_scalar(
                tsb[:, :], xa[:, xp, :], lns[:, 2:3],
                lns[:, 6:7], ALU.subtract, ALU.mult), c1, h1, last_t_user)
            last_x_users[xp] = [i1]
            last_t_user = p.dma("ac", lambda E, T=T: E.dma_start(
                Tb[T * 128:(T + 1) * 128, :], tsb[:, :]), i1)
            tags_tb[T] = last_t_user
            if T % 2 == 0 and T >= 2:
                i = T // 2 - 1
                if i + 2 <= 5:
                    run_conversions({f"e{i + 2}"})
                if i <= 3:
                    run_fills([f"e{i}"])

            if t == 1:
                half0 = T - 1
                tpar = 0
                deps = [tags_tb[half0 + i] for i in range(2)] + tt_users[tpar]
                tt_users[tpar] = []
                tts = []
                for c in range(16):
                    tts.append(p.dma("sy", lambda E, c=c, h0=half0, tp2=tpar:
                                     E.dma_start(
                        TTh[:, tp2, c, :],
                        Tb[h0 * 128:h0 * 128 + 256, c * 128:(c + 1) * 128],
                        transpose=True), *deps))
                tt_tag = tts[-1]

                for tp in range(T - 1, T + 1):
                    ti = tp % 2
                    def proj(E, ti=ti, tpar=tpar):
                        ins = None
                        for c in range(16):
                            stat = TTh[:, tpar, c, ti * 128:(ti + 1) * 128]
                            for q in range(4):
                                ins = E.matmul(bank(q), stat,
                                               WkT[:, c, q * 512:(q + 1) * 512],
                                               start=(c == 0), stop=False)
                            for q in range(4):
                                ins = E.matmul(bank(4 + q), stat,
                                               WvT[:, c, q * 512:(q + 1) * 512],
                                               start=(c == 0), stop=False)
                        for q in range(4):
                            ins = E.matmul(bank(q), ones_bf[:, 0:128],
                                           biasrow_kv[:, q * 512:(q + 1) * 512],
                                           start=False, stop=True)
                        for q in range(4):
                            ins = E.matmul(bank(4 + q), ones_bf[:, 0:128],
                                           biasrow_kv[:, 2048 + q * 512:
                                                      2048 + (q + 1) * 512],
                                           start=False, stop=True)
                        return ins
                    t_proj = p.pe(proj, tt_tag, t_ones, t_bk, t_bv,
                                  tag_wk, tag_wv, *prev_evac)
                    tt_users[tpar].append(t_proj)
                    t_kb = p.ac(lambda E, tp=tp: E.activation(
                        Kb[:, :], psA[:, 0:2048], AF.Relu,
                        scale=decay[:, tp:tp + 1]), t_proj, t_dec)
                    t_vb = p.dv(lambda E: E.tensor_copy(
                        Vb3[:, :, 0:128],
                        psA[:, 2048:4096].rearrange("p (h e) -> p h e", e=128)),
                        t_proj, t_vones)
                    def kvmm(E):
                        ins = None
                        for h in range(NH):
                            ins = E.matmul(kvslot(h), Kb[:, h * 128:(h + 1) * 128],
                                           r2(Vb3[:, h:h + 1, 0:129]),
                                           start=True, stop=True)
                        return ins
                    t_kv = p.pe(kvmm, t_kb, t_vb)
                    src_a = psA[:, 0:5 * 512].rearrange(
                        "p (bk x) -> p bk x", x=512)[:, :, 0:396].rearrange(
                        "p bk (s e) -> p bk s e", e=132)[:, :, :, 0:129]
                    dst_a = KVacc.rearrange(
                        "p (bk s) e -> p bk s e", s=3)[:, 0:5, :, 0:129]
                    src_b = psA[:, 5 * 512:5 * 512 + 129]
                    dst_b = r2(KVacc[:, 15:16, 0:129])
                    if tp == 0:
                        ka = p.dv(lambda E, s=src_a, d=dst_a: E.tensor_copy(d, s),
                                  t_kv)
                        kb2 = p.dv(lambda E, s=src_b, d=dst_b: E.tensor_copy(d, s),
                                   t_kv)
                    else:
                        ka = p.dv(lambda E, s=src_a, d=dst_a: E.tensor_tensor(
                            d, s, d, ALU.add), t_kv, t_kvacc)
                        kb2 = p.dv(lambda E, s=src_b, d=dst_b: E.tensor_tensor(
                            d, s, d, ALU.add), t_kv, t_kvacc)
                    t_kvacc = kb2
                    prev_evac = [t_kb, t_vb, ka, kb2]

        t_kvs = p.dv(lambda E: E.tensor_copy(KVS[:, 0:16, :], KVacc[:, 0:16, :]),
                     t_kvacc)
        if _DBG:
            p.dma("ac", lambda E: E.dma_start(dbg["kvacc"][:, :], r2(KVacc)),
                  t_kvacc)

        p.barrier()
        p.emit_block()

    if upto == 'a1':
        t_last = p.dma("ac", lambda E: E.dma_start(
            out_d[0:128, 0:1980], r2(KVacc[:, 0:15, :])), t_kvacc)
        p.wait("sy", t_last)
        p.emit_block()
        attn_es.close()
        ges.close()
        p._stack.close()
        return nc

    with contextlib.ExitStack() as a2s:
        def al(name, shape, dt):
            return a2s.enter_context(nc.sbuf_tensor(name, shape, dt))

        wAB = al("wAB", [128, 16, 2048], BF16)
        QrT = al("QrT", [128, 16, 1024], BF16)
        tho = al("tho", [128, 2, 16, 512], BF16)
        xa = al("xa2", [128, 2048], F32)
        Oraw = al("Oraw", [128, 16, 132], F32)
        Oz = al("Oz", [128, 2, 2048], BF16)
        x1sb = al("x1sb", [128, 2048], F32)
        aow = al("aow", [128, 2048], F32)
        tfb = al("tfb", [128, 2048], BF16)
        sqd = tfb
        lns = al("lns2", [128, 8], F32)
        zz = al("zz", [128, 16], F32)

        wq_users, wo_users = [], []
        tho_users = [[], []]
        last_x_users = []
        last_x1_user = None
        last_tf_user = None
        last_oz_users = [None, None]
        qrt_war = []
        prev_q_evac = [None, None]
        oraw_users = []
        or_evac_tag = None
        ao_prev = [None, None]
        prev_d1 = None

        for g in range(G):
            tags_w = [p.dma("sy", lambda E: E.dma_start(wAB[:, :, :], tWa[0]),
                            fill_tag("Wq"), *wo_users)]
            wo_users = []
            qrt_tags = []
            for half in range(2):
                h0 = g * 1024 + half * 512
                tts = []
                for c in range(16):
                    tts.append(p.dma("sy", lambda E, c=c, h0=h0, hp=half:
                                     E.dma_start(
                        tho[:, hp, c, :], Tb[h0:h0 + 512, c * 128:(c + 1) * 128],
                        transpose=True), *tho_users[half]))
                tt_tag = tts[-1]
                tho_users[half] = []
                for dj in range(16):
                    bk_i = dj % 2
                    def qmm(E, dj=dj, bk_i=bk_i, hp=half):
                        ins = None
                        for c in range(16):
                            ins = E.matmul(bank(bk_i),
                                           wAB[:, c, dj * 128:(dj + 1) * 128],
                                           tho[:, hp, c, :], start=(c == 0),
                                           stop=(c == 15))
                        return ins
                    t_q = p.pe(qmm, tt_tag, tags_w[-1], prev_q_evac[bk_i])
                    tho_users[half].append(t_q)
                    wq_users.append(t_q)
                    t_qe = p.ac(lambda E, dj=dj, half=half, bk_i=bk_i:
                                E.activation(
                                    QrT[:, dj, half * 512:(half + 1) * 512],
                                    bank(bk_i), AF.Relu,
                                    bias=bq_pp[:, dj:dj + 1]),
                                t_q, t_bqp, *qrt_war)
                    qrt_war = []
                    prev_q_evac[bk_i] = t_qe
                    qrt_tags.append(t_qe)
            if _DBG and g == 0:
                p.dma("ac", lambda E: E.dma_start(dbg["qrt"][:, :], r2(QrT)),
                      *qrt_tags)
            if upto == 'a2q':
                t_last = p.dma("ac", lambda E: E.dma_start(
                    dbg["qrt"][:, :], r2(QrT)), *qrt_tags)
                p.wait("sy", t_last)
                p.emit_block()
                return nc
            wo_tags = [p.dma("sy", lambda E: E.dma_start(wAB[:, :, :], tWa[3]),
                             fill_tag("Wo"), *wq_users)]
            wq_users = []
            tags_ob = {}
            for t in range(GT):
                T = g * GT + t
                def omm(E, t=t):
                    ins = None
                    for h in range(NH):
                        ins = E.matmul(kvslot(h),
                                       QrT[:, h, t * 128:(t + 1) * 128],
                                       r2(KVS[:, h:h + 1, 0:129]),
                                       start=True, stop=True)
                    return ins
                t_o = p.pe(omm, qrt_tags[-1], t_kvs, or_evac_tag, *oraw_users)
                qrt_war.append(t_o)
                oraw_users = []
                ev = []
                for h in range(NH):
                    ev.append(p.ac(lambda E, h=h: E.copy(
                        r2(Oraw[:, h:h + 1, 0:129]), kvslot(h)), t_o))
                or_evac_tag = ev[-1]
                z1 = p.dv(lambda E, T=T: E.tensor_scalar(
                    zz[:, :], Oraw[:, :, 128], decay[:, T:T + 1], AT_EPS,
                    ALU.mult, ALU.add), ev[-1], t_dec)
                z2 = p.dv(lambda E: E.reciprocal(zz[:, :], zz[:, :]), z1)
                z3 = p.dv(lambda E, T=T: E.tensor_scalar(
                    zz[:, :], zz[:, :], decay[:, T:T + 1], None, ALU.mult), z2)
                par = t % 2
                oz_t = []
                for h in range(NH):
                    oz_t.append(p.dv(lambda E, h=h, par=par: E.tensor_scalar(
                        Oz[:, par, h * 128:(h + 1) * 128],
                        r2(Oraw[:, h:h + 1, 0:128]),
                        zz[:, h:h + 1], None, ALU.mult),
                        z3, last_oz_users[par]))
                oraw_users = [oz_t[-1]]
                t_obw = p.dma("ac", lambda E, T=T, par=par: E.dma_start(
                    Ob[T * 128:(T + 1) * 128, :], Oz[:, par, :]), *oz_t)
                last_oz_users[par] = t_obw
                tags_ob[t] = t_obw
            if upto == 'a2o':
                t_last = p.dma("ac", lambda E: E.dma_start(
                    dbg["tf"][0:1024, :], Ob[0:1024, :]), tags_ob[7])
                p.wait("sy", t_last)
                p.emit_block()
                return nc
            for half in range(4):
                h0 = g * 1024 + half * 256
                qpar2 = half % 2
                ots = []
                for c in range(16):
                    ots.append(p.dma("sy", lambda E, c=c, h0=h0, hp=qpar2:
                                     E.dma_start(
                        tho[:, hp, c, 0:256],
                        Ob[h0:h0 + 256, c * 128:(c + 1) * 128],
                        transpose=True),
                        tags_ob[half * 2 + 1], *tho_users[qpar2]))
                ot_tag = ots[-1]
                tho_users[qpar2] = []
                for tt_ in range(2):
                    t = half * 2 + tt_
                    T = g * GT + t
                    par = t % 2
                    b0 = par * 4
                    def aomm(E, tt_=tt_, b0=b0, hp=qpar2):
                        ins = None
                        for c in range(16):
                            stat = tho[:, hp, c, tt_ * 128:(tt_ + 1) * 128]
                            for q in range(4):
                                ins = E.matmul(bank(b0 + q), stat,
                                               wAB[:, c, q * 512:(q + 1) * 512],
                                               start=(c == 0), stop=False)
                        for q in range(4):
                            ins = E.matmul(bank(b0 + q), ones_bf[:, 0:128],
                                           biasrow_o[:, q * 512:(q + 1) * 512],
                                           start=False, stop=True)
                        return ins
                    t_ao = p.pe(aomm, ot_tag, wo_tags[-1], t_ones, t_bo,
                                ao_prev[par], or_evac_tag)
                    tho_users[qpar2].append(t_ao)
                    wo_users.append(t_ao)
                    aops = psA[:, b0 * 512:(b0 + 4) * 512]
                    if upto == 'a2w2' and g == 0 and t == 0:
                        import os
                        stage = int(os.environ.get('A2W_STAGE', '0'))
                        tags = [t_ao]
                        if stage >= 1:
                            tags.append(p.dv(lambda E: E.reduce_sum(
                                lns[:, 0:1], aops, axis=AX.X), t_ao))
                        if stage >= 2:
                            tags.append(p.ac(lambda E: E.activation(
                                sqd[:, :], aops, AF.Square,
                                accum_out=lns[:, 1:2]), t_ao))
                        if stage >= 3:
                            c1x = p.dv(lambda E: E.tensor_scalar(
                                lns[:, 2:3], lns[:, 0:1], RD, None, ALU.mult),
                                tags[1])
                            d1x = p.dv(lambda E: E.tensor_scalar(
                                lns[:, 3:4], lns[:, 1:2], RD, None, ALU.mult),
                                tags[2])
                            e1x = p.dv(lambda E: E.tensor_tensor(
                                lns[:, 4:5], lns[:, 2:3], lns[:, 2:3],
                                ALU.mult), c1x)
                            f1x = p.dv(lambda E: E.tensor_tensor(
                                lns[:, 4:5], lns[:, 3:4], lns[:, 4:5],
                                ALU.subtract), d1x, e1x)
                            tags += [c1x, d1x, e1x, f1x]
                        if stage >= 4:
                            g1x = p.ac(lambda E: E.activation(
                                lns[:, 5:6], lns[:, 4:5], AF.Sqrt,
                                bias=epsc[:, :]), f1x, t_eps)
                            h1x = p.dv(lambda E: E.reciprocal(
                                lns[:, 6:7], lns[:, 5:6]), g1x)
                            tags += [g1x, h1x]
                        if stage >= 5:
                            tags.append(p.dv(lambda E: E.tensor_scalar(
                                aow[:, :], aops, lns[:, 2:3], lns[:, 6:7],
                                ALU.subtract, ALU.mult), c1x, h1x))
                        tcp = p.dv(lambda E, aops=aops: E.tensor_copy(
                            aow[:, :], aops), *tags)
                        t_last = p.dma("ac", lambda E: E.dma_start(
                            dbg["x1"][0:128, :], aow[:, :]), tcp)
                        p.wait("sy", t_last)
                        p.emit_block()
                        return nc
                    tcp = p.dv(lambda E, aops=aops: E.tensor_copy(
                        aow[:, :], aops), t_ao)
                    ao_prev[par] = tcp
                    c1, h1, prev_d1, _ = ln_stats(aow[:, :], lns, sqd,
                                                  [tcp, last_tf_user], prev_d1)
                    h2 = p.dv(lambda E: E.tensor_scalar(
                        aow[:, :], aow[:, :], lns[:, 2:3], lns[:, 6:7],
                        ALU.subtract, ALU.mult), c1, h1)
                    if upto == 'a2w' and g == 0 and t == 0:
                        t_last = p.dma("ac", lambda E: E.dma_start(
                            dbg["x1"][0:128, :], aow[:, :]), h2)
                        p.wait("sy", t_last)
                        p.emit_block()
                        return nc
                    t_x = p.dma("ac", lambda E, T=T: E.dma_start(
                        xa[:, :], x_d[T * 128:(T + 1) * 128, :]), *last_x_users)
                    i1 = p.dv(lambda E: E.tensor_tensor(x1sb[:, :], aow[:, :],
                                                        xa[:, :], ALU.add),
                              h2, t_x, last_x1_user)
                    last_x_users = [i1]
                    t_x1w = p.dma("ac", lambda E, T=T: E.dma_start(
                        X1[T * 128:(T + 1) * 128, :], x1sb[:, :]), i1)
                    if _DBG:
                        p.dma("ac", lambda E, T=T: E.dma_start(
                            dbg["x1"][T * 128:(T + 1) * 128, :], x1sb[:, :]), i1)
                    c1, h1, prev_d1, _ = ln_stats(x1sb[:, :], lns, sqd, [i1],
                                                  prev_d1)
                    j1 = p.dv(lambda E: E.tensor_scalar(
                        tfb[:, :], x1sb[:, :], lns[:, 2:3], lns[:, 6:7],
                        ALU.subtract, ALU.mult), c1, h1, last_tf_user)
                    last_x1_user = j1
                    t_tfw = p.dma("ac", lambda E, T=T: E.dma_start(
                        Tf[T * 128:(T + 1) * 128, :], tfb[:, :]), j1)
                    if _DBG:
                        p.dma("ac", lambda E, T=T: E.dma_start(
                            dbg["tf"][T * 128:(T + 1) * 128, :], tfb[:, :]), j1)
                    last_tf_user = t_tfw
                    tags_tf[T] = t_tfw
            if g < 2:
                run_fills([f"e{4 + g}"])
        p.barrier()
        p.emit_block()
    attn_es.close()

    if upto == 'a2':
        t_last = p.dma("ac", lambda E: E.dma_start(
            out_d[:, :], X1[:, :]), tags_tf[NT - 1])
        p.wait("sy", t_last)
        p.emit_block()
        ges.close()
        p._stack.close()
        return nc

    with contextlib.ExitStack() as f_s:
        def al(name, shape, dt):
            return f_s.enter_context(nc.sbuf_tensor(name, shape, dt))

        TTf = al("TTf", [128, 16, 1024], BF16)
        Ysb = al("Ysb", [128, 8, 2048], F32)
        Hsb = al("Hsb", [128, 11, 1024], BF16)
        w13 = [[al(f"w13_{w}_{i}", [128, 16, 256], BF16) for i in range(2)]
               for w in range(2)]
        w2q = [al(f"w2q_{i}", [128, 11, 512], BF16) for i in range(2)]
        silu = al("silu", [128, 1, 1024], BF16)
        weF = al("weF", [128, 128], F32)
        glog = al("glog", [128, 8, 16], F32)
        gm = al("gm", [128, 4], F32)
        gmk = al("gmk", [128, 3, 16], F32)
        gateT = al("gateT", [128, 16, 16], BF16)
        x1a = al("x1a", [128, 2048], F32)
        outsb = al("outsb", [128, 2048], F32)
        sqd = outsb
        lns = al("lns3", [128, 8], F32)

        JBS = [(0, 2), (2, 2), (4, 2), (6, 2), (8, 2), (10, 1)]

        w13_users = [[], []]
        w2_users = [[], []]
        h_users = []
        y_evac = [None, None]
        h_evac = {}
        ttf_users = []
        gate_users = []
        y_users = []
        last_x1_users = []
        last_out_user = None
        out_tags = []
        sil_users = [[], []]
        prev_d1 = None
        prev_wf = None

        for g in range(G):
            g0 = g * 1024
            ttfs = []
            for c in range(16):
                ttfs.append(p.dma("sy", lambda E, c=c, g0=g0: E.dma_start(
                    TTf[:, c, :], Tf[g0:g0 + 1024, c * 128:(c + 1) * 128],
                    transpose=True), tags_tf[g * GT + 7], *ttf_users))
            ttf_users = []
            ttf_tag = ttfs[-1]
            gts = []
            for c in range(16):
                gts.append(p.dma("sy", lambda E, c=c: E.dma_start(
                    gateT[:, c, :], cgw[:, c * 128:(c + 1) * 128],
                    transpose=True), t_gwc2, *gate_users))
            gate_users = []
            we_tags = []
            prev_g = None
            for t in range(GT):
                def gmm(E, t=t):
                    ins = None
                    for c in range(16):
                        ins = E.matmul(bank(6, 0, 16),
                                       TTf[:, c, t * 128:(t + 1) * 128],
                                       gateT[:, c, :], start=(c == 0),
                                       stop=(c == 15))
                    return ins
                t_g = p.pe(gmm, ttf_tag, gts[-1], prev_g)
                gate_users.append(t_g)
                ttf_users.append(t_g)
                ge = p.ac(lambda E, t=t: E.copy(glog[:, t, :], bank(6, 0, 16)),
                          t_g)
                prev_g = ge
                m0 = p.dv(lambda E, t=t: E.reduce_max(gm[:, 0:1], glog[:, t, :],
                                                      axis=AX.X), ge)
                m1 = p.dv(lambda E: E.tensor_scalar(gm[:, 1:2], gm[:, 0:1], -1.0,
                                                    None, ALU.mult), m0)
                ex = p.ac(lambda E, t=t: E.activation(gmk[:, 0, :], glog[:, t, :],
                                                      AF.Exp, bias=gm[:, 1:2]),
                          m1, prev_wf)
                s0 = p.dv(lambda E: E.reduce_sum(gm[:, 2:3], gmk[:, 0, :],
                                                 axis=AX.X), ex)
                s1 = p.dv(lambda E: E.reciprocal(gm[:, 2:3], gm[:, 2:3]), s0)
                pr = p.dv(lambda E: E.tensor_scalar(gmk[:, 0, :], gmk[:, 0, :],
                                                    gm[:, 2:3], None, ALU.mult),
                          s1)
                r1 = p.dv(lambda E: E.reduce_max(gm[:, 0:1], gmk[:, 0, :],
                                                 axis=AX.X), pr)
                k1 = p.dv(lambda E: E.tensor_scalar(gmk[:, 1, :], gmk[:, 0, :],
                                                    gm[:, 0:1], None, ALU.is_ge),
                          r1)
                k2 = p.dv(lambda E: E.tensor_scalar(gmk[:, 2, :], gmk[:, 1, :],
                                                    -1.0, 1.0, ALU.mult, ALU.add),
                          k1)
                p2 = p.dv(lambda E: E.tensor_tensor(gmk[:, 2, :], gmk[:, 0, :],
                                                    gmk[:, 2, :], ALU.mult), k2)
                r2_ = p.dv(lambda E: E.reduce_max(gm[:, 1:2], gmk[:, 2, :],
                                                  axis=AX.X), p2)
                k3 = p.dv(lambda E: E.tensor_scalar(gmk[:, 2, :], gmk[:, 2, :],
                                                    gm[:, 1:2], None, ALU.is_ge),
                          r2_)
                k4 = p.dv(lambda E: E.tensor_tensor(gmk[:, 1, :], gmk[:, 1, :],
                                                    gmk[:, 2, :], ALU.add), k3)
                wf = p.dv(lambda E, t=t: E.tensor_tensor(
                    weF[:, t * 16:(t + 1) * 16], gmk[:, 0, :], gmk[:, 1, :],
                    ALU.mult), k4, pr)
                we_tags.append(wf)
                prev_wf = wf
            if _DBG and g < 2:
                p.dma("ac", lambda E, g=g: E.dma_start(
                    dbg["we"][:, g * 128:(g + 1) * 128], weF[:, :]), *we_tags)

            for e in range(18):
                if g == 0 and 2 <= e <= 13:
                    run_conversions({f"e{e + 4}"})
                if g == 0 and 4 <= e <= 15:
                    run_fills([f"e{e + 2}"])
                w13_tags = {}

                def load_w13(bi, e=e):
                    par = bi % 2
                    deps = [e_fill_tag(e)] + w13_users[par]
                    w13_users[par] = []
                    tag = p.dma("sy", lambda E, bi=bi, par=par, e=e:
                                E.dma_start(w13[0][par][:, :, :],
                                            tw13[e, 0, bi]), *deps)
                    tag = p.dma("sy", lambda E, bi=bi, par=par, e=e:
                                E.dma_start(w13[1][par][:, :, :],
                                            tw13[e, 1, bi]), *deps)
                    w13_tags[bi] = tag

                load_w13(0)
                load_w13(1)
                for bi, (j0, njs) in enumerate(JBS):
                    par = bi % 2
                    if bi >= 2:
                        pass
                    for jj in range(njs):
                        j = j0 + jj
                        jpar = j % 2
                        b0 = jpar * 4
                        def hmm(E, jj=jj, par=par, b0=b0):
                            ins = None
                            for c in range(16):
                                s1_ = w13[0][par][:, c, jj * 128:(jj + 1) * 128]
                                s3_ = w13[1][par][:, c, jj * 128:(jj + 1) * 128]
                                ins = E.matmul(bank(b0 + 0), s1_,
                                               TTf[:, c, 0:512],
                                               start=(c == 0), stop=(c == 15))
                                ins = E.matmul(bank(b0 + 1), s1_,
                                               TTf[:, c, 512:1024],
                                               start=(c == 0), stop=(c == 15))
                                ins = E.matmul(bank(b0 + 2), s3_,
                                               TTf[:, c, 0:512],
                                               start=(c == 0), stop=(c == 15))
                                ins = E.matmul(bank(b0 + 3), s3_,
                                               TTf[:, c, 512:1024],
                                               start=(c == 0), stop=(c == 15))
                            return ins
                        extra = [we_tags[-1]] if (e == 0 and bi == 0) else []
                        t_h = p.pe(hmm, ttf_tag, w13_tags[bi],
                                   h_evac.get(j - 2), h_evac.get(j - 1),
                                   *(y_evac if (e > 0 or g > 0) else []),
                                   *extra, *h_users)
                        h_users = []
                        w13_users[par].append(t_h)
                        ttf_users.append(t_h)
                        t_sil = p.ac(lambda E, b0=b0: E.activation(
                            silu[:, 0, :], psA[:, b0 * 512:(b0 + 2) * 512],
                            AF.Silu), t_h, *sil_users[0])
                        sil_users[0] = []
                        t_he = p.dv(lambda E, j=j, b0=b0:
                                    E.tensor_tensor(
                                        Hsb[:, j, :], silu[:, 0, :],
                                        psA[:, (b0 + 2) * 512:(b0 + 4) * 512],
                                        ALU.mult), t_sil, h_evac.get(j))
                        sil_users[0].append(t_he)
                        h_evac[j] = t_he
                    if bi + 2 < len(JBS):
                        load_w13(bi + 2)
                if _DBG and g == 1 and e == 0:
                    p.dma("ac", lambda E: E.dma_start(
                        dbg["h"][:, :], r2(Hsb)), h_evac[10], h_evac[9])
                    p.dma("ac", lambda E: E.dma_start(
                        dbg["ttf"][:, :], r2(TTf)), h_evac[10])
                for dq in range(4):
                    qpar = (e * 4 + dq) % 2
                    deps = [e_fill_tag(e)] + w2_users[qpar]
                    w2_users[qpar] = []
                    w2tag = p.dma("sy", lambda E, dq=dq, qpar=qpar, e=e:
                                  E.dma_start(w2q[qpar][:, :, :],
                                              tw2q[e, dq]), *deps)
                    for t in range(GT):
                        ypar = t % 2
                        def ymm(E, t=t, qpar=qpar, ypar=ypar):
                            ins = None
                            for j in range(NJ):
                                ins = E.matmul(bank(ypar),
                                               Hsb[:, j, t * 128:(t + 1) * 128],
                                               w2q[qpar][:, j, :],
                                               start=(j == 0),
                                               stop=(j == NJ - 1))
                            return ins
                        t_y = p.pe(ymm, w2tag, h_evac[NJ - 1], h_evac[NJ - 2],
                                   y_evac[ypar])
                        w2_users[qpar].append(t_y)
                        h_users.append(t_y)
                        ydst = Ysb[:, t, dq * 512:(dq + 1) * 512]
                        wecol = (weF[:, t * 16 + e:t * 16 + e + 1]
                                 if e < NEXP else None)
                        if e == 0:
                            t_ye = p.dv(lambda E, ydst=ydst, wecol=wecol,
                                        ypar=ypar: E.tensor_scalar(
                                            ydst, bank(ypar), wecol, None,
                                            ALU.mult),
                                        t_y, we_tags[t], *y_users)
                        elif e < NEXP:
                            t_ye = p.dv(lambda E, ydst=ydst, wecol=wecol,
                                        ypar=ypar: E.scalar_tensor_tensor(
                                            ydst, bank(ypar), wecol, ydst,
                                            ALU.mult, ALU.add),
                                        t_y, we_tags[t])
                        else:
                            t_ye = p.dv(lambda E, ydst=ydst, ypar=ypar:
                                        E.tensor_tensor(ydst, bank(ypar), ydst,
                                                        ALU.add), t_y)
                        y_evac[ypar] = t_ye
                y_users = []
            if _DBG and g < 2:
                p.dma("ac", lambda E, g=g: E.dma_start(
                    dbg["y"][:, g * 16384:(g + 1) * 16384], r2(Ysb)),
                    y_evac[0], y_evac[1])
            for t in range(GT):
                T = g * GT + t
                ysl = Ysb[:, t, :]
                c1, h1, prev_d1, btag = ln_stats(ysl, lns, sqd,
                                                 [y_evac[0], y_evac[1],
                                                  last_out_user],
                                                 prev_d1)
                y_users.append(btag)
                t_x1 = p.dma("ac", lambda E, T=T: E.dma_start(
                    x1a[:, :], X1[T * 128:(T + 1) * 128, :]), *last_x1_users)
                h2 = p.dv(lambda E, ysl=ysl: E.tensor_scalar(
                    outsb[:, :], ysl, lns[:, 2:3], lns[:, 6:7],
                    ALU.subtract, ALU.mult), c1, h1, last_out_user)
                y_users.append(h2)
                i1 = p.dv(lambda E: E.tensor_tensor(outsb[:, :], outsb[:, :],
                                                    x1a[:, :], ALU.add),
                          h2, t_x1)
                y_users.append(i1)
                last_x1_users = [i1]
                t_ow = p.dma("ac", lambda E, T=T: E.dma_start(
                    out_d[T * 128:(T + 1) * 128, :], outsb[:, :]), i1)
                last_out_user = t_ow
                out_tags.append(t_ow)
        p.wait("sy", out_tags[-1])
        p.wait("ac", out_tags[-1])
        p.emit_block()

    ges.close()
    p._stack.close()
    return nc


_IN_MAP = {
    "L0_attn_Wq": "Wq", "L0_attn_Wk": "Wk", "L0_attn_Wv": "Wv",
    "L0_attn_Wo": "Wo", "L0_attn_bq": "bq", "L0_attn_bk": "bk",
    "L0_attn_bv": "bv", "L0_attn_bo": "bo", "L0_ffn_gate_w": "gw",
    "L0_ffn_w1": "w1", "L0_ffn_w2": "w2", "L0_ffn_w3": "w3",
    "L0_ffn_sw1": "sw1", "L0_ffn_sw2": "sw2", "L0_ffn_sw3": "sw3",
}


def kernel(**inputs):
    x = np.ascontiguousarray(np.asarray(inputs["L0_attn_x"], dtype=np.float32))
    common = {v: np.ascontiguousarray(np.asarray(inputs[k], dtype=np.float32))
              for k, v in _IN_MAP.items()}
    nc = build()
    in_maps = []
    for i in range(8):
        m = dict(common)
        m["x"] = x[i]
        in_maps.append(m)
    res = run_bass_kernel_spmd(nc, in_maps, core_ids=list(range(8)))
    return np.stack([res.results[i]["out"] for i in range(8)], axis=0)
